# version 09 (final)
# speedup vs baseline: 1.6223x; 1.1102x over previous
import numpy as np
import concourse.bass as bass
import concourse.mybir as mybir
from concourse.bass_utils import run_bass_kernel_spmd
from contextlib import ExitStack
import ml_dtypes

F32 = mybir.dt.float32
BF16 = mybir.dt.bfloat16
AF = mybir.ActivationFunctionType
ALU = mybir.AluOpType
AX = mybir.AxisListType.X

T = 4096
D = 1024
NDC = 8
SEG = 1024
NSEG = T // SEG
LSUB = 8
NH, NKV, HD = 16, 4, 64
NE = 8
EPS_LN = 1e-5
EPS_RMS = 1e-6
bfdt = ml_dtypes.bfloat16

ENGINES = ("sync", "vector", "scalar", "tensor", "gpsimd")


class Ctr:
    def __init__(self):
        self.v = 0


class Prog:
    def __init__(self):
        self.q = {e: [] for e in ENGINES}

    def emit(self, eng, fn):
        self.q[eng].append(fn)

    def wait(self, eng, sem, ctr_or_val):
        v = ctr_or_val.v if isinstance(ctr_or_val, Ctr) else ctr_or_val
        if v > 0:
            self.emit(eng, lambda e, sem=sem, v=v: e.wait_ge(sem, v))


class Arena:
    def __init__(self, nbytes):
        self.nbytes = nbytes
        self.regions = {}
        self.free_list = [(0, nbytes)]

    def alloc(self, name, size, align=64, top=False):
        assert name not in self.regions
        order = reversed(list(enumerate(self.free_list))) if top else enumerate(self.free_list)
        for i, (off, sz) in order:
            if top:
                a = (off + sz - size) // align * align
                if a < off:
                    continue
            else:
                a = (off + align - 1) // align * align
                if a + size > off + sz:
                    continue
            newfree = []
            if a > off:
                newfree.append((off, a - off))
            if off + sz > a + size:
                newfree.append((a + size, off + sz - (a + size)))
            self.free_list[i:i + 1] = newfree
            self.regions[name] = (a, size)
            return a
        raise MemoryError(f"arena: cannot alloc {name} ({size}B); free={self.free_list}")

    def free(self, name):
        off, size = self.regions.pop(name)
        self.free_list.append((off, size))
        self.free_list.sort()
        out = []
        for o, s in self.free_list:
            if out and out[-1][0] + out[-1][1] == o:
                out[-1] = (out[-1][0], out[-1][1] + s)
            else:
                out.append((o, s))
        self.free_list = out


class _StopBuild(Exception):
    pass


def build_nc(dbg=(), stop_after=None):
    nc = bass.Bass("TRN2", target_bir_lowering=False)
    dt = nc.dram_tensor

    x_fm = dt("x_fm", [D, T], F32, kind="ExternalInput").ap()
    w_in = dt("w_in", [D, 2 * D], F32, kind="ExternalInput").ap()
    dt_w = dt("dt_w", [D, D], F32, kind="ExternalInput").ap()
    b_w = dt("b_w", [D, 16], F32, kind="ExternalInput").ap()
    c_w = dt("c_w", [D, 16], F32, kind="ExternalInput").ap()
    w_out = dt("w_out", [D, D], F32, kind="ExternalInput").ap()
    wqkv_t = dt("wqkv_t", [D, 1536], F32, kind="ExternalInput").ap()
    wo_t = dt("wo_t", [D, D], F32, kind="ExternalInput").ap()
    gate_t = dt("gate_t", [D, NE], F32, kind="ExternalInput").ap()
    expert_t = dt("expert_t", [NE, D, D], F32, kind="ExternalInput").ap()
    vec_pack = dt("vec_pack", [128, 8 * NDC], F32, kind="ExternalInput").ap()
    conv_w4 = dt("conv_w4", [128, NDC * 4], F32, kind="ExternalInput").ap()
    ident_bf = dt("ident_bf", [128, 128], BF16, kind="ExternalInput").ap()
    bstack_mask = dt("bstack_mask", [128, 256], BF16, kind="ExternalInput").ap()
    repmat = dt("repmat", [16, 128], BF16, kind="ExternalInput").ap()
    cstack_mask = dt("cstack_mask", [128, 256], BF16, kind="ExternalInput").ap()
    gmask = dt("gmask", [1, LSUB * LSUB], F32, kind="ExternalInput").ap()
    diag_mask = dt("diag_mask", [128, 4 * 512], BF16, kind="ExternalInput").ap()

    out_fm = dt("out_fm", [D, T], F32, kind="ExternalOutput").ap()

    xn_s = dt("xn_s", [D, T], BF16, kind="Internal").ap()
    z_s = dt("z_s", [D, T], BF16, kind="Internal").ap()
    xsp_s = dt("xsp_s", [D, T], BF16, kind="Internal").ap()
    xs_s = dt("xs_s", [D, T], BF16, kind="Internal").ap()
    macc_s = dt("macc_s", [D, T], BF16, kind="Internal").ap()
    h_s = dt("h_s", [D, T], BF16, kind="Internal").ap()

    dbg_list = {}

    def dbg_t(name, shape, dtype=F32):
        if name in dbg:
            ap = dt("dbg_" + name, shape, dtype, kind="ExternalOutput").ap()
            dbg_list[name] = ap
            return ap
        return None

    with ExitStack() as st:
        AR_BYTES = 196 * 1024
        ar_h = st.enter_context(nc.sbuf_tensor("arena", [128, AR_BYTES // 2], BF16))
        ar_bf_full = ar_h[:]
        ar_f32_full = ar_h[:].bitcast(F32)
        ar = Arena(AR_BYTES)

        def vbf(off, cols):
            assert off % 2 == 0
            return ar_bf_full[:, off // 2: off // 2 + cols]

        def vf32(off, cols):
            assert off % 4 == 0
            return ar_f32_full[:, off // 4: off // 4 + cols]

        NSLOT = 14
        stats_h = st.enter_context(nc.sbuf_tensor("stats", [128, NSLOT * 32], F32))
        statsF = stats_h[:]
        statsB = stats_h[:].bitcast(BF16)

        def scol(i, tc=None):
            if tc is None:
                return statsF[:, i * 32:(i + 1) * 32]
            return statsF[:, i * 32 + tc:i * 32 + tc + 1]

        def scol_bf(i, tc=None):
            if tc is None:
                return statsB[:, i * 64:i * 64 + 32]
            return statsB[:, i * 64 + tc:i * 64 + tc + 1]

        rowscr_h = st.enter_context(nc.sbuf_tensor("rowscr", [1, T], BF16))
        ROWSCR = rowscr_h[:]

        psA_h = st.enter_context(nc.psum_tensor("psA", [128, 2048], F32))
        psB_h = st.enter_context(nc.psum_tensor("psB", [128, 1024], F32))
        psC_h = st.enter_context(nc.psum_tensor("psC", [128, 512], F32))
        psD_h = st.enter_context(nc.psum_tensor("psD", [128, 512], F32))
        psA, psB, psC, psD = psA_h[:], psB_h[:], psC_h[:], psD_h[:]

        sDs = st.enter_context(nc.semaphore("sDs"))
        sDg = st.enter_context(nc.semaphore("sDg"))
        sV = st.enter_context(nc.semaphore("sV"))
        sAc = st.enter_context(nc.semaphore("sAc"))
        sT = st.enter_context(nc.semaphore("sT"))
        sP = st.enter_context(nc.semaphore("sP"))
        cDs, cDg, cV, cA, cT, cP = Ctr(), Ctr(), Ctr(), Ctr(), Ctr(), Ctr()
        SEM = dict(vector=(sV, cV), scalar=(sAc, cA), tensor=(sT, cT), gpsimd=(sP, cP))

        P = Prog()

        def _mk(eng):
            sem, ctr = SEM.get(eng, (None, None))

            def op(fn, inc=True):
                if inc:
                    ctr.v += 1
                    P.emit(eng, lambda e, fn=fn: fn(e).then_inc(sem, 1))
                else:
                    P.emit(eng, lambda e, fn=fn: fn(e))
            return op

        v_op, a_op, t_op, p_op = _mk("vector"), _mk("scalar"), _mk("tensor"), _mk("gpsimd")

        def dma_s(fn):
            cDs.v += 16
            P.emit("sync", lambda e, fn=fn: fn(e).then_inc(sDs, 16))

        def dma_g(fn):
            cDg.v += 16
            P.emit("gpsimd", lambda e, fn=fn: fn(e).then_inc(sDg, 16))

        def barrier(engines=("vector", "scalar", "tensor", "gpsimd")):
            marks = [(SEM[e][0], SEM[e][1].v) for e in engines]
            for e in engines:
                for sem, val in marks:
                    if SEM[e][0] is not sem:
                        P.wait(e, sem, val)

        co = ar.alloc("consts", 8 * 1024)
        IDENT = vbf(co, 128)
        BSTM = vbf(co + 256, 256)
        CSTM = vbf(co + 768, 256)
        DIAGM = vbf(co + 1280, 2048)
        VECP = vf32(co + 5376, 64)
        CONVW = vf32(co + 5632, 32)
        NEGA = vf32(co + 5760, NDC)
        ONESC = vbf(co + 5792, 1)
        EPSLN_C = vf32(co + 5824, 1)
        EPSRMS_C = vf32(co + 5856, 1)
        IDENTF = vf32(co + 5888, 128)

        repm_h = st.enter_context(nc.sbuf_tensor("repm", [16, 128], BF16))
        REPM = repm_h[:]
        gmask_h = st.enter_context(nc.sbuf_tensor("gmaskt", [1, LSUB * LSUB], F32))
        GMASK = gmask_h[:]
        ones1_h = st.enter_context(nc.sbuf_tensor("ones1", [1, 128], BF16))
        ONES1 = ones1_h[:]

        dma_s(lambda e: e.dma_start(out=IDENT, in_=ident_bf))
        dma_s(lambda e: e.dma_start(out=BSTM, in_=bstack_mask))
        dma_s(lambda e: e.dma_start(out=CSTM, in_=cstack_mask))
        dma_s(lambda e: e.dma_start(out=DIAGM, in_=diag_mask))
        dma_s(lambda e: e.dma_start(out=VECP, in_=vec_pack))
        dma_s(lambda e: e.dma_start(out=CONVW, in_=conv_w4))
        dma_s(lambda e: e.dma_start(out=REPM, in_=repmat))
        dma_s(lambda e: e.dma_start(out=GMASK, in_=gmask))
        consts_dma = cDs.v

        def veccol(idx, dc):
            return VECP[:, idx * NDC + dc: idx * NDC + dc + 1]

        P.wait("vector", sDs, consts_dma)
        P.wait("tensor", sDs, consts_dma)
        P.wait("gpsimd", sDs, consts_dma)
        v_op(lambda e: e.memset(ONES1, 1.0))
        v_op(lambda e: e.memset(ONESC, 1.0))
        v_op(lambda e: e.memset(EPSLN_C, EPS_LN))
        v_op(lambda e: e.memset(EPSRMS_C, EPS_RMS))
        v_op(lambda e: e.tensor_copy(IDENTF, IDENT))
        P.wait("scalar", sDs, consts_dma)
        a_op(lambda e: e.activation(NEGA, VECP[:, 4 * NDC:5 * NDC], AF.Exp))
        mk = cA.v
        P.wait("vector", sAc, mk)
        v_op(lambda e: e.tensor_scalar_mul(NEGA, NEGA, -1.0))
        init_v = cV.v

        xoff = ar.alloc("x_bf", 64 * 1024)
        X = [vbf(xoff + dc * 8192, T) for dc in range(NDC)]
        for dc in range(NDC):
            dma_g(lambda e, dc=dc: e.dma_start(out=X[dc], in_=x_fm[dc * 128:(dc + 1) * 128, :]))
        x_loaded = cDg.v

        x2off = ar.alloc("x2_tmp", top=True, size=8 * 1024)
        X2 = vbf(x2off, T)

        S_SX, S_SX2, S_MU, S_R, S_RRMS, S_MUL, S_MUSC, S_S12, S_IRR, S_RG = range(10)
        S_T1, S_T2, S_BF1, S_BF2 = 10, 11, 12, 13

        P.wait("tensor", sDg, x_loaded)
        P.wait("tensor", sV, init_v)
        P.wait("scalar", sDg, x_loaded)
        for dc in range(NDC):
            a_op(lambda e, dc=dc: e.activation(X2, X[dc], AF.Square))
            amk = cA.v
            P.wait("tensor", sAc, amk)
            for tc in range(32):
                first = (dc == 0 and tc == 0)
                last = (dc == NDC - 1 and tc == 31)
                t_op(lambda e, dc=dc, tc=tc, first=first, last=last: e.matmul(
                    psC[:, tc * 8 + dc:tc * 8 + dc + 1],
                    X[dc][:, tc * 128:(tc + 1) * 128], ONESC,
                    start=first, stop=last), inc=False)
                t_op(lambda e, dc=dc, tc=tc, first=first, last=last: e.matmul(
                    psD[:, tc * 8 + dc:tc * 8 + dc + 1],
                    X2[:, tc * 128:(tc + 1) * 128], ONESC,
                    start=first, stop=last), inc=(tc == 31))
            if dc < NDC - 1:
                P.wait("scalar", sT, cT.v)
        tmk = cT.v
        P.wait("vector", sT, tmk)
        if "pscdump" in dbg:
            ap = dbg_t("pscdump", [128, 512])
            v_op(lambda e: e.tensor_copy(vf32(x2off, 512), psC))
            P.wait("sync", sV, cV.v)
            dma_s(lambda e, ap=ap: e.dma_start(out=ap, in_=vf32(x2off, 512)))
            P.wait("vector", sDs, cDs.v)
        v_op(lambda e: e.reduce_sum(
            scol(S_SX).rearrange("p (a b) -> p a b", b=1),
            psC[:, 0:256].rearrange("p (a b) -> p a b", b=8), axis=AX))
        v_op(lambda e: e.reduce_sum(
            scol(S_SX2).rearrange("p (a b) -> p a b", b=1),
            psD[:, 0:256].rearrange("p (a b) -> p a b", b=8), axis=AX))

        v_op(lambda e: e.tensor_scalar_mul(scol(S_MU), scol(S_SX), 1.0 / D))
        v_op(lambda e: e.tensor_scalar_mul(scol(S_T1), scol(S_SX2), 1.0 / D))
        v_op(lambda e: e.tensor_mul(scol(S_T2), scol(S_MU), scol(S_MU)))
        v_op(lambda e: e.tensor_sub(scol(S_T1), scol(S_T1), scol(S_T2)))
        vmk = cV.v
        P.wait("scalar", sV, vmk)
        a_op(lambda e: e.activation(scol(S_T2), scol(S_T1), AF.Ln, bias=EPSLN_C))
        a_op(lambda e: e.activation(scol(S_R), scol(S_T2), AF.Exp, scale=-0.5))
        amk = cA.v
        P.wait("vector", sAc, amk)
        v_op(lambda e: e.tensor_mul(scol(S_T2), scol(S_R), scol(S_R)))
        v_op(lambda e: e.tensor_mul(scol(S_T2), scol(S_T1), scol(S_T2)))
        vmk = cV.v
        P.wait("scalar", sV, vmk)
        a_op(lambda e: e.activation(scol(S_T1), scol(S_T2), AF.Ln, bias=EPSRMS_C))
        a_op(lambda e: e.activation(scol(S_RRMS), scol(S_T1), AF.Exp, scale=-0.5))
        a_op(lambda e: e.activation(scol(S_IRR), scol(S_T1), AF.Exp, scale=0.5))
        a_op(lambda e: e.activation(scol(S_SX), scol(S_T2), AF.Ln, bias=EPSLN_C))
        a_op(lambda e: e.activation(scol(S_SX2), scol(S_SX), AF.Exp, scale=-0.5))
        amk = cA.v
        P.wait("vector", sAc, amk)
        v_op(lambda e: e.tensor_mul(scol(S_MUL), scol(S_R), scol(S_RRMS)))
        v_op(lambda e: e.tensor_mul(scol(S_MUSC), scol(S_MU), scol(S_MUL)))
        v_op(lambda e: e.tensor_mul(scol(S_S12), scol(S_SX2), scol(S_IRR)))
        stats_done = cV.v
        if "statsearly" in dbg:
            ap = dbg_t("statsearly", [128, 448])
            P.wait("sync", sV, stats_done)
            dma_s(lambda e, ap=ap: e.dma_start(out=ap, in_=statsF))
            P.wait("vector", sDs, cDs.v)

        bco = ar.alloc("bcast", 24 * 1024)
        MULBC = vbf(bco, T)
        MUSCBC = vbf(bco + 8192, T)
        IRRBC = vbf(bco + 16384, T)

        rowify_calls = [0]

        def rowify_and_bcast(slot, dstt):
            rowify_calls[0] += 1
            this_call = rowify_calls[0]
            """stat column slot -> ROWSCR bf16 row -> broadcast tile dstt [128, T] bf16"""
            v_op(lambda e: e.tensor_copy(scol_bf(S_BF1), scol(slot)))
            vmk = cV.v
            P.wait("tensor", sV, vmk)
            for g in range(8):
                for j in range(4):
                    t_op(lambda e, g=g, j=j: e.matmul(
                        psB[0:1, j * 128:(j + 1) * 128],
                        scol_bf(S_BF1, g * 4 + j), IDENT,
                        start=(j == 0), stop=(j == 3)), inc=(j == 3))
                tmk = cT.v
                P.wait("vector", sT, tmk)
                v_op(lambda e, g=g: e.tensor_copy(ROWSCR[:, g * 512:(g + 1) * 512], psB[0:1, 0:512]))
                vmk = cV.v
                P.wait("tensor", sV, vmk)
            if this_call == 1 and "rowscr" in dbg:
                ap = dbg_t("rowscr", [1, T], BF16)
                P.wait("sync", sV, cV.v)
                dma_s(lambda e, ap=ap: e.dma_start(out=ap, in_=ROWSCR))
                api = dbg_t("identdump", [128, 128], BF16)
                if api is not None:
                    dma_s(lambda e, api=api: e.dma_start(out=api, in_=IDENT))
                aps = dbg_t("statsdump", [128, 448])
                if aps is not None:
                    dma_s(lambda e, aps=aps: e.dma_start(out=aps, in_=statsF))
                P.wait("tensor", sDs, cDs.v)
            for tc in range(8):
                t_op(lambda e, tc=tc: e.matmul(
                    psA[:, (tc % 4) * 512:(tc % 4) * 512 + 512], ONES1,
                    ROWSCR[:, tc * 512:(tc + 1) * 512], start=True, stop=True))
                tmk = cT.v
                P.wait("vector", sT, tmk)
                v_op(lambda e, dstt=dstt, tc=tc: e.tensor_copy(
                    dstt[:, tc * 512:(tc + 1) * 512], psA[:, (tc % 4) * 512:(tc % 4) * 512 + 512]))
                vmk = cV.v
                P.wait("tensor", sV, vmk)

        rowify_and_bcast(S_MUL, MULBC)
        rowify_and_bcast(S_MUSC, MUSCBC)
        rowify_and_bcast(S_IRR, IRRBC)
        bcast_done = cV.v

        xnoff = ar.alloc("xn_rms", 64 * 1024)
        XN = [vbf(xnoff + dc * 8192, T) for dc in range(NDC)]
        XNP = vbf(x2off, T)
        for dc in range(NDC):
            v_op(lambda e, dc=dc: e.tensor_mul(XN[dc], X[dc], MULBC))
            v_op(lambda e, dc=dc: e.tensor_sub(XN[dc], XN[dc], MUSCBC))
            v_op(lambda e, dc=dc: e.tensor_mul(XNP, XN[dc], IRRBC))
            vmk = cV.v
            P.wait("sync", sV, vmk)
            dma_s(lambda e, dc=dc: e.dma_start(out=xn_s[dc * 128:(dc + 1) * 128, :], in_=XNP))
            dmk = cDs.v
            P.wait("vector", sDs, dmk)
        xn_done = cV.v
        if "mulbc" in dbg:
            for nm, tile in (("mulbc", MULBC), ("muscbc", MUSCBC), ("irrbc", IRRBC)):
                ap = dbg_t(nm, [128, T], BF16)
                if ap is not None:
                    P.wait("sync", sV, xn_done)
                    dma_s(lambda e, ap=ap, tile=tile: e.dma_start(out=ap, in_=tile))
        ar.free("x_bf")
        ar.free("x2_tmp")
        ar.free("bcast")

        try:
            if stop_after == "p1":
                P.emit("sync", lambda e, v=cDs.v: e.wait_ge(sDs, v))
                raise _StopBuild
            s12off = ar.alloc("s12bc", top=True, size=8 * 1024)
            S12BC = vbf(s12off, T)
            rowify_and_bcast(S_S12, S12BC)

            wsl_off = ar.alloc("wslab", 32 * 1024)
            WSLAB = [[vbf(wsl_off + buf * 16 * 1024 + cc * 2048, 1024) for cc in range(NDC)]
                     for buf in range(2)]
            koff = ar.alloc("kfm", 32 * 1024)
            KFZ = [vbf(koff + i * 8192, T) for i in range(4)]
            voff = ar.alloc("vtm", 17 * 1024)
            VT = [vbf(voff + tt * 520, 260) for tt in range(32)]
            qstg_off = ar.alloc("qstg", top=True, size=8 * 1024)
            QSTG = vbf(qstg_off, T)
            mstg_off = ar.alloc("mstg", top=True, size=8 * 1024)
            MSTG = vbf(mstg_off, T)
            bwc_off = ar.alloc("bwcol", top=True, size=256)
            BWCOL = vf32(bwc_off, 16)

            q_s = dt("q_s", [D, T], BF16, kind="Internal").ap()

            vmk_list = []
            for tt in range(32):
                v_op(lambda e, tt=tt: e.memset(
                    VT[tt].rearrange("p (kv c) -> p kv c", kv=4)[:, :, 64:65], 1.0), inc=(tt == 31))
            v_op(lambda e: e.memset(KFZ[0][64:128, :], 0.0))
            v_op(lambda e: e.memset(KFZ[1][0:64, :], 0.0))
            v_op(lambda e: e.memset(KFZ[2][64:128, :], 0.0))
            v_op(lambda e: e.memset(KFZ[3][0:64, :], 0.0))
            vt_ones = cV.v

            GCOLS = {0: 6, 1: 6, 2: 0, 3: 0}

            SLAB_SRC = {0: (wqkv_t, 0, 1024), 1: (wqkv_t, 1024, 512),
                        2: (w_in, 0, 1024), 3: (w_in, 1024, 1024)}

            def load_slab(buf, g):
                src_t, soff, width = SLAB_SRC[g]
                for cc in range(NDC):
                    dma_g(lambda e, cc=cc, soff=soff, width=width, buf=buf, src_t=src_t: e.dma_start(
                        out=WSLAB[buf][cc][:, 0:width],
                        in_=src_t[cc * 128:(cc + 1) * 128, soff:soff + width]))
                return cDg.v

            def scale_slab(buf, g, width):
                for cc in range(NDC):
                    v_op(lambda e, cc=cc, buf=buf, g=g, width=width: e.tensor_scalar_mul(
                        WSLAB[buf][cc][:, 0:width], WSLAB[buf][cc][:, 0:width],
                        veccol(GCOLS[g], cc)))
                return cV.v

            P.wait("gpsimd", sV, xn_done)
            ld0 = load_slab(0, 0)
            P.wait("vector", sDg, ld0)
            sc0 = scale_slab(0, 0, 1024)
            P.wait("tensor", sV, sc0)
            ld1 = load_slab(1, 1)
            SCL8 = vf32(co + 6464, 1)
            v_op(lambda e: e.memset(SCL8, 0.125))
            scl8_v = cV.v
            P.wait("scalar", sV, scl8_v)
            if stop_after == "prep":
                raise _StopBuild
            for mc in range(8):
                for tc in range(8):
                    bank = psA[:, (tc % 4) * 512:(tc % 4) * 512 + 512]
                    for cc in range(NDC):
                        t_op(lambda e, mc=mc, tc=tc, cc=cc, bank=bank: e.matmul(
                            bank, WSLAB[0][cc][:, mc * 128:(mc + 1) * 128],
                            XN[cc][:, tc * 512:(tc + 1) * 512],
                            start=(cc == 0), stop=(cc == NDC - 1)))
                    tmk = cT.v
                    P.wait("scalar", sT, tmk)
                    a_op(lambda e, tc=tc, bank=bank: e.activation(
                        QSTG[:, tc * 512:(tc + 1) * 512], bank, AF.Identity, scale=SCL8))
                    amk = cA.v
                    P.wait("tensor", sAc, amk)
                P.wait("sync", sAc, cA.v)
                dma_s(lambda e, mc=mc: e.dma_start(out=q_s[mc * 128:(mc + 1) * 128, :], in_=QSTG))
                P.wait("scalar", sDs, cDs.v)

            if stop_after == "qloop":
                raise _StopBuild
            P.wait("vector", sDg, ld1)
            sc1 = scale_slab(1, 1, 512)
            P.wait("tensor", sV, sc1)
            P.wait("gpsimd", sT, cT.v)
            ld2 = load_slab(0, 2)
            for mc in range(2):
                for tc in range(8):
                    bank = psA[:, (tc % 4) * 512:(tc % 4) * 512 + 512]
                    for cc in range(NDC):
                        t_op(lambda e, mc=mc, tc=tc, cc=cc, bank=bank: e.matmul(
                            bank, WSLAB[1][cc][:, mc * 128:(mc + 1) * 128],
                            XN[cc][:, tc * 512:(tc + 1) * 512],
                            start=(cc == 0), stop=(cc == NDC - 1)))
                    tmk = cT.v
                    P.wait("vector", sT, tmk)
                    v_op(lambda e, mc=mc, tc=tc, bank=bank: e.tensor_copy(
                        KFZ[2 * mc][0:64, tc * 512:(tc + 1) * 512], bank[0:64, :]))
                    v_op(lambda e, mc=mc, tc=tc, bank=bank: e.tensor_copy(
                        KFZ[2 * mc + 1][64:128, tc * 512:(tc + 1) * 512], bank[64:128, :]))
                    P.wait("tensor", sV, cV.v)
            P.wait("vector", sV, vt_ones)
            for tt in range(32):
                bank = psB[:, (tt % 2) * 512:(tt % 2) * 512 + 256]
                for cc in range(NDC):
                    t_op(lambda e, tt=tt, cc=cc, bank=bank: e.matmul(
                        bank, XN[cc][:, tt * 128:(tt + 1) * 128],
                        WSLAB[1][cc][:, 256:512],
                        start=(cc == 0), stop=(cc == NDC - 1)))
                tmk = cT.v
                P.wait("vector", sT, tmk)
                v_op(lambda e, tt=tt, bank=bank: e.tensor_copy(
                    VT[tt].rearrange("p (kv c) -> p kv c", kv=4)[:, :, 0:64],
                    bank.rearrange("p (kv c) -> p kv c", kv=4)))
                P.wait("tensor", sV, cV.v)
            qkv_done_v = cV.v
            qkv_done_a = cA.v

            winproj_done = cDs.v
            conv_done = cDs.v
            if stop_after != "qkv":
                bcol_scr = vbf(co + 6528, 1)
                eps1_off = ar.alloc("eps1", top=True, size=4096)
                EPS1 = vf32(eps1_off, 512)
                EPS1B = vf32(eps1_off + 2048, 512)
                ld3 = None
                for g in (2, 3):
                    buf = g - 2
                    ld = ld2 if g == 2 else ld3
                    P.wait("vector", sDg, ld)
                    P.wait("tensor", sDg, ld)
                    bw_ps = psD if g == 2 else psC
                    for cc in range(NDC):
                        v_op(lambda e, cc=cc: e.tensor_copy(bcol_scr, veccol(1, cc)))
                        P.wait("tensor", sV, cV.v)
                        for mc in range(8):
                            t_op(lambda e, cc=cc, mc=mc, buf=buf, bw_ps=bw_ps: e.matmul(
                                bw_ps[:, mc:mc + 1],
                                WSLAB[buf][cc][:, mc * 128:(mc + 1) * 128], bcol_scr,
                                start=(cc == 0 and mc == 0),
                                stop=(cc == NDC - 1 and mc == 7)), inc=(mc == 7))
                        P.wait("vector", sT, cT.v)
                    P.wait("vector", sT, cT.v)
                    v_op(lambda e, g=g, bw_ps=bw_ps: e.tensor_copy(
                        BWCOL[:, (g - 2) * 8:(g - 2) * 8 + 8], bw_ps[:, 0:8]))
                    bw_done = cV.v
                    scg = scale_slab(buf, g, 1024)
                    P.wait("tensor", sV, scg)
                    if g == 2:
                        P.wait("gpsimd", sT, cT.v)
                        ld3 = load_slab(1, 3)
                    P.wait("scalar", sV, bw_done)
                    dst = xsp_s if g == 2 else z_s
                    wbank_mark = {}
                    weps_mark = {}
                    for mc in range(8):
                        gm = (g - 2) * 8 + mc
                        for tc in range(8):
                            it = mc * 8 + tc
                            bank = psA[:, (tc % 4) * 512:(tc % 4) * 512 + 512]
                            P.wait("tensor", sV, wbank_mark.get(it - 4, 0))
                            for cc in range(NDC):
                                t_op(lambda e, mc=mc, cc=cc, bank=bank, buf=buf, tc=tc: e.matmul(
                                    bank, WSLAB[buf][cc][:, mc * 128:(mc + 1) * 128],
                                    XN[cc][:, tc * 512:(tc + 1) * 512],
                                    start=(cc == 0), stop=(cc == NDC - 1)))
                            tmk = cT.v
                            P.wait("vector", sT, tmk)
                            P.wait("vector", sAc, weps_mark.get(it - 2, 0))
                            ep = EPS1 if it % 2 == 0 else EPS1B
                            v_op(lambda e, tc=tc, bank=bank, ep=ep: e.tensor_mul(
                                ep, bank, S12BC[:, tc * 512:(tc + 1) * 512]))
                            wbank_mark[it] = cV.v
                            P.wait("scalar", sV, cV.v)
                            a_op(lambda e, tc=tc, gm=gm, ep=ep: e.activation(
                                MSTG[:, tc * 512:(tc + 1) * 512], ep, AF.Identity,
                                bias=BWCOL[:, gm:gm + 1]))
                            weps_mark[it] = cA.v
                        P.wait("sync", sAc, cA.v)
                        dma_s(lambda e, mc=mc, dst=dst: e.dma_start(
                            out=dst[mc * 128:(mc + 1) * 128, :], in_=MSTG))
                        P.wait("scalar", sDs, cDs.v)
                winproj_done = cDs.v
                ar.free("wslab")
                ar.free("s12bc")
                ar.free("qstg")

                cvb_off = ar.alloc("convbuf", top=True, size=4 * 1024)
                XSPB = [vbf(cvb_off + b * 2048, 515 + 1) for b in range(2)]
                ctm_off = ar.alloc("convterm", top=True, size=2 * 1024)
                CTERM = [vbf(ctm_off + b * 1024, 512) for b in range(2)]
                XSOUT = MSTG
                buf_free = [0, 0]
                bank_free = [0, 0]
                for dc in range(NDC):
                    v_op(lambda e, dc=dc: e.memset(XSPB[0][:, 0:3], 0.0))
                    v_op(lambda e, dc=dc: e.memset(XSPB[1][:, 0:3], 0.0))
                    pend_v = cV.v
                    P.wait("sync", sV, pend_v)
                    for ch in range(8):
                        bb = ch % 2
                        P.wait("sync", sV, buf_free[bb])
                        if ch == 0:
                            dma_s(lambda e, dc=dc: e.dma_start(
                                out=XSPB[0][:, 3:515],
                                in_=xsp_s[dc * 128:(dc + 1) * 128, 0:512]))
                        else:
                            dma_s(lambda e, dc=dc, ch=ch, bb=bb: e.dma_start(
                                out=XSPB[bb][:, 0:515],
                                in_=xsp_s[dc * 128:(dc + 1) * 128, ch * 512 - 3:ch * 512 + 512]))
                        dmk = cDs.v
                        P.wait("vector", sDs, dmk)
                        bank = psC if ch % 2 == 0 else psD
                        P.wait("tensor", sAc, bank_free[ch % 2])
                        for k in range(4):
                            v_op(lambda e, dc=dc, k=k, bb=bb: e.tensor_scalar_mul(
                                CTERM[k % 2], XSPB[bb][:, k:k + 512], CONVW[:, dc * 4 + k:dc * 4 + k + 1]))
                            vmk = cV.v
                            P.wait("tensor", sV, vmk)
                            t_op(lambda e, k=k, bank=bank: e.matmul(
                                bank, IDENT, CTERM[k % 2], start=(k == 0), stop=(k == 3)))
                            if k >= 1:
                                P.wait("vector", sT, cT.v)
                        buf_free[bb] = cV.v
                        tmk = cT.v
                        P.wait("scalar", sT, tmk)
                        a_op(lambda e, dc=dc, ch=ch, bank=bank: e.activation(
                            XSOUT[:, ch * 512:(ch + 1) * 512], bank, AF.Silu, bias=veccol(2, dc)))
                        bank_free[ch % 2] = cA.v
                        amk = cA.v
                        P.wait("tensor", sAc, amk)
                    P.wait("sync", sAc, cA.v)
                    dma_s(lambda e, dc=dc: e.dma_start(
                        out=xs_s[dc * 128:(dc + 1) * 128, :], in_=XSOUT))
                    P.wait("scalar", sDs, cDs.v)
                conv_done = cDs.v
                ar.free("convbuf")
                ar.free("convterm")
                ar.free("mstg")
                ar.free("xn_rms")
                ar.free("bwcol")
                ar.free("eps1")


        except _StopBuild:
            pass
        if stop_after is None:
            ofoff = ar.alloc("ofm", 64 * 1024)
            OFM = [vbf(ofoff + i * 8192, T) for i in range(8)]
            qoff = ar.alloc("qload", top=True, size=16 * 1024)
            QBUF = [vbf(qoff + i * 8192, T) for i in range(2)]
            pboff = ar.alloc("pbuf", top=True, size=8 * 1024)
            PBUF = [vbf(pboff + i * 4096, 2048) for i in range(2)]
            rloff = ar.alloc("rlscr", top=True, size=2 * 1024)
            RLS = vf32(rloff, 512)
            rlboff = ar.alloc("rlb", top=True, size=1024)
            RLB = vbf(rlboff, 512)
            oscoff = ar.alloc("oscr", top=True, size=2 * 1024)
            OSC = vf32(oscoff, 512)
            onroff = ar.alloc("onesrow", top=True, size=256)
            ONESROW = vbf(onroff, 64)

            dma_s(lambda e: e.dma_start(out=QBUF[0], in_=q_s[0:128, :]))
            q_loads = [cDs.v, 0]
            q_used = [0, 0]
            v_op(lambda e: e.memset(ONESROW, 1.0))
            SELB = vbf(onroff + 128, 64)
            v_op(lambda e: e.memset(SELB, 0.0))
            v_op(lambda e: e.memset(SELB[64:65, :], 1.0))
            v_op(lambda e: e.memset(RLB, 0.0))
            onesrow_v = cV.v
            P.wait("tensor", sV, onesrow_v)

            prev_evac = [0, 0]
            prev_av_done = [0]
            HPAIRS = [(0, 4), (1, 5), (2, 6), (3, 7), (8, 12), (9, 13), (10, 14), (11, 15)]
            for hi in range(NH):
                hp = hi // 2
                pos = hi % 2
                h = HPAIRS[hp][pos]
                if pos == 0:
                    if hp < 7:
                        P.wait("sync", sT, q_used[(hp + 1) % 2])
                        dma_s(lambda e, hp=hp: e.dma_start(
                            out=QBUF[(hp + 1) % 2], in_=q_s[(hp + 1) * 128:(hp + 2) * 128, :]))
                        q_loads[(hp + 1) % 2] = cDs.v
                    P.wait("tensor", sDs, q_loads[hp % 2])
                kv = h // 4
                KT = KFZ[kv]
                QT = QBUF[hp % 2]
                odd = pos
                vt_lo = kv * 65
                for qr in range(8):
                    nkc = (qr + 1) * 4
                    ngroups = (nkc + 1) // 2
                    exp_mark = {}
                    av_mark = {}
                    first_av = True

                    def emit_qk(g):
                        kcs = [k for k in (2 * g, 2 * g + 1) if k < nkc]
                        half = g % 2
                        if g >= 2:
                            P.wait("tensor", sAc, exp_mark[g - 2])
                        mmfns = []
                        for ci, kc in enumerate(kcs):
                            slot = psA[:, half * 1024 + ci * 512: half * 1024 + ci * 512 + 512]
                            diag = kc >= qr * 4
                            mmfns.append(lambda e, kc=kc, slot=slot, KT=KT, QT=QT,
                                         qr=qr, diag=diag: e.matmul(
                                slot, KT[:, kc * 128:(kc + 1) * 128],
                                QT[:, qr * 512:(qr + 1) * 512],
                                start=True, stop=not diag))
                            if diag:
                                mmfns.append(lambda e, slot=slot, kc=kc, qr=qr: e.matmul(
                                    slot, IDENT,
                                    DIAGM[:, (kc - qr * 4) * 512:(kc - qr * 4) * 512 + 512],
                                    start=False, stop=True))
                        for fn in mmfns[:-1]:
                            t_op(fn, inc=False)
                        t_op(mmfns[-1])
                        return cT.v, kcs, half

                    def emit_exp(g, sc_done, kcs, half):
                        P.wait("scalar", sT, sc_done)
                        if g >= 2:
                            P.wait("scalar", sT, av_mark[g - 2])
                        elif prev_av_done[0] > 0:
                            P.wait("scalar", sT, prev_av_done[0])
                        width = 512 * len(kcs)
                        a_op(lambda e, half=half, width=width: e.activation(
                            PBUF[half][:, 0:width], psA[:, half * 1024:half * 1024 + width],
                            AF.Exp))
                        exp_mark[g] = cA.v

                    def emit_av(g, kcs, half, last_group):
                        nonlocal first_av
                        if first_av:
                            P.wait("tensor", sV, prev_evac[0])
                            P.wait("tensor", sAc, prev_evac[1])
                        P.wait("tensor", sAc, exp_mark[g])
                        avfns = []
                        for ci, kc in enumerate(kcs):
                            last_overall = (last_group and ci == len(kcs) - 1)
                            avfns.append(lambda e, kc=kc, half=half, ci=ci, st=first_av,
                                         sp=last_overall, vt_lo=vt_lo: e.matmul(
                                psB[0:65, 0:512], VT[kc][:, vt_lo:vt_lo + 65],
                                PBUF[half][:, ci * 512:ci * 512 + 512],
                                start=st, stop=sp))
                            first_av = False
                        for fn in avfns[:-1]:
                            t_op(fn, inc=False)
                        t_op(avfns[-1])
                        av_mark[g] = cT.v

                    ginfo = {}
                    for g in range(ngroups):
                        sc_done, kcs, half = emit_qk(g)
                        ginfo[g] = (kcs, half)
                        emit_exp(g, sc_done, kcs, half)
                        if g >= 1:
                            kcs_p, half_p = ginfo[g - 1]
                            emit_av(g - 1, kcs_p, half_p, False)
                    kcs_l, half_l = ginfo[ngroups - 1]
                    emit_av(ngroups - 1, kcs_l, half_l, True)
                    av_done = cT.v
                    prev_av_done[0] = av_done
                    P.wait("scalar", sT, av_done)
                    a_op(lambda e: e.activation(RLS[64:65, :], psB[64:65, 0:512], AF.Ln))
                    a_op(lambda e: e.activation(RLB[64:65, :], RLS[64:65, :], AF.Exp, scale=-1.0))
                    a_op(lambda e: e.activation(OSC[0:64, :], psB[0:64, 0:512], AF.Identity))
                    amk = cA.v
                    P.wait("tensor", sAc, amk)
                    t_op(lambda e: e.matmul(psB[0:64, 512:1024], SELB,
                                            RLB, start=True, stop=True))
                    tmk = cT.v
                    P.wait("vector", sT, tmk)
                    v_op(lambda e, hp=hp, qr=qr, odd=odd: e.tensor_mul(
                        OFM[hp][odd * 64:odd * 64 + 64, qr * 512:(qr + 1) * 512],
                        OSC[0:64, :], psB[0:64, 512:1024]))
                    prev_evac[0] = cV.v
                    prev_evac[1] = cA.v
                if pos == 1:
                    q_used[hp % 2] = cT.v
            attn_done_v = cV.v
            ar.free("qload")
            ar.free("pbuf")
            ar.free("rlscr")
            ar.free("rlb")
            ar.free("kfm")
            ar.free("vtm")

            hoff = ar.alloc("h", 64 * 1024)
            HT = [vbf(hoff + i * 8192, T) for i in range(NDC)]
            woboff = ar.alloc("wob", top=True, size=16 * 1024)
            WOB = [vbf(woboff + cc * 2048, 1024) for cc in range(NDC)]
            xnroff = ar.alloc("xnr", top=True, size=16 * 1024)
            XNR = [vbf(xnroff + i * 8192, T) for i in range(2)]
            hsqoff = ar.alloc("hsq", top=True, size=8 * 1024)
            HSQ = vbf(hsqoff, T)

            P.wait("gpsimd", sV, attn_done_v)
            for cc in range(NDC):
                dma_g(lambda e, cc=cc: e.dma_start(out=WOB[cc], in_=wo_t[cc * 128:(cc + 1) * 128, :]))
            wo_loaded = cDg.v
            P.wait("sync", sV, attn_done_v)
            dma_s(lambda e: e.dma_start(out=XNR[0], in_=xn_s[0:128, :]))
            P.wait("tensor", sDg, wo_loaded)
            P.wait("tensor", sV, attn_done_v)
            xnr_loads = [cDs.v, 0]
            xnr_used = [0, 0]
            h2_first = True
            for mc in range(NDC):
                if mc < NDC - 1:
                    P.wait("sync", sV, xnr_used[(mc + 1) % 2])
                    dma_s(lambda e, mc=mc: e.dma_start(
                        out=XNR[(mc + 1) % 2], in_=xn_s[(mc + 1) * 128:(mc + 2) * 128, :]))
                    xnr_loads[(mc + 1) % 2] = cDs.v
                P.wait("vector", sDs, xnr_loads[mc % 2])
                for tc in range(8):
                    bank = psA[:, (tc % 4) * 512:(tc % 4) * 512 + 512]
                    for cc in range(NDC):
                        t_op(lambda e, mc=mc, tc=tc, cc=cc, bank=bank: e.matmul(
                            bank, WOB[cc][:, mc * 128:(mc + 1) * 128],
                            OFM[cc][:, tc * 512:(tc + 1) * 512],
                            start=(cc == 0), stop=(cc == NDC - 1)))
                    tmk = cT.v
                    P.wait("vector", sT, tmk)
                    v_op(lambda e, mc=mc, tc=tc, bank=bank: e.tensor_add(
                        HT[mc][:, tc * 512:(tc + 1) * 512], bank,
                        XNR[mc % 2][:, tc * 512:(tc + 1) * 512]))
                    vmk = cV.v
                    P.wait("tensor", sV, vmk)
                xnr_used[mc % 2] = cV.v
                P.wait("scalar", sV, cV.v)
                a_op(lambda e, mc=mc: e.activation(HSQ, HT[mc], AF.Square))
                amk = cA.v
                P.wait("tensor", sAc, amk)
                for tcc in range(32):
                    first = h2_first
                    h2_first = False
                    last = (mc == NDC - 1 and tcc == 31)
                    t_op(lambda e, tcc=tcc, first=first, last=last: e.matmul(
                        psD[:, tcc:tcc + 1], HSQ[:, tcc * 128:(tcc + 1) * 128], ONESC,
                        start=first, stop=last), inc=(tcc == 31))
                if mc < NDC - 1:
                    P.wait("scalar", sT, cT.v)
            t_op(lambda e: e.matmul(psC[0:1, 0:128], ONESC, IDENT, start=True, stop=True))
            wo_done_t = cT.v
            P.wait("vector", sT, wo_done_t)
            if "psddump" in dbg:
                ap = dbg_t("psddump", [128, 64])
                v_op(lambda e: e.tensor_copy(vf32(hsqoff, 64), psD[:, 0:64]))
                P.wait("sync", sV, cV.v)
                dma_s(lambda e, ap=ap: e.dma_start(out=ap, in_=vf32(hsqoff, 64)))
                P.wait("vector", sDs, cDs.v)
            v_op(lambda e: e.tensor_scalar_mul(scol(S_RG), psD[:, 0:32], 1.0 / D))
            vmk = cV.v
            P.wait("scalar", sV, vmk)
            a_op(lambda e: e.activation(scol(S_T1), scol(S_RG), AF.Ln, bias=EPSRMS_C))
            a_op(lambda e: e.activation(scol(S_RG), scol(S_T1), AF.Exp, scale=-0.5))
            rg_done = cA.v
            P.wait("sync", sV, cV.v)
            for mc in range(NDC):
                dma_s(lambda e, mc=mc: e.dma_start(
                    out=h_s[mc * 128:(mc + 1) * 128, :], in_=HT[mc]))
            h_spilled = cDs.v
            ar.free("wob")
            ar.free("xnr")
            ar.free("hsq")
            ar.free("oscr")
            ar.free("onesrow")

            if "ofm0" in dbg:
                ap = dbg_t("ofm0", [128, T], BF16)
                P.wait("sync", sV, attn_done_v)
                dma_s(lambda e, ap=ap: e.dma_start(out=ap, in_=OFM[0]))
            if "ofmall" in dbg:
                P.wait("sync", sV, attn_done_v)
                for jj in range(8):
                    apj = dbg_t("ofmall" + str(jj), [128, T], BF16)
                    dma_s(lambda e, apj=apj, jj=jj: e.dma_start(out=apj, in_=OFM[jj]))
            if "h0" in dbg:
                ap = dbg_t("h0", [128, T], BF16)
                P.wait("sync", sV, cV.v)
                dma_s(lambda e, ap=ap: e.dma_start(out=ap, in_=HT[0]))
            if "h7" in dbg:
                ap = dbg_t("h7", [128, T], BF16)
                P.wait("sync", sV, cV.v)
                dma_s(lambda e, ap=ap: e.dma_start(out=ap, in_=HT[7]))
            if "rgcol" in dbg:
                ap = dbg_t("rgcol", [128, 32])
                P.wait("sync", sAc, rg_done)
                dma_s(lambda e, ap=ap: e.dma_start(out=ap, in_=scol(S_RG)))

        if stop_after is None:
            ar.free("h")
            ar.free("ofm")
            dtwoff = ar.alloc("dtw", 16 * 1024)
            DTW = [vbf(dtwoff + cc * 2048, 1024) for cc in range(NDC)]
            wooff2 = ar.alloc("wout", 16 * 1024)
            WOUT = [vbf(wooff2 + cc * 2048, 1024) for cc in range(NDC)]
            bcwoff = ar.alloc("bcw", 1024, top=True)
            BWT = [vbf(bcwoff + cc * 32, 16) for cc in range(NDC)]
            CWT = [vbf(bcwoff + 256 + cc * 32, 16) for cc in range(NDC)]
            xsoff = ar.alloc("xseg", 16 * 1024)
            XSG = [vbf(xsoff + cc * 2048, SEG) for cc in range(NDC)]
            ygoff = ar.alloc("yg", 16 * 1024)
            YG = [vbf(ygoff + cc * 2048, SEG) for cc in range(NDC)]
            gbcoff = ar.alloc("gbc", 16 * 1024)
            GBC = [vbf(gbcoff + k * 2048, SEG) for k in range(LSUB)]
            b8off = ar.alloc("b8s", 8 * 1024)
            B8S = vf32(b8off, 2048)
            hsoff = ar.alloc("hsub", 8448)
            HSUB = vf32(hsoff, 2064 + 48)
            def atop(name, size):
                return ar.alloc(name, size, top=True)
            SC_E = vf32(atop("sc_e", 4096), SEG)
            DTT = vf32(atop("dtt", 4096), SEG)
            LDEC = vf32(atop("ldec", 4096), SEG)
            S128 = vf32(atop("s128", 4096), SEG)
            SC1 = vf32(atop("sc1", 512), 128)
            TMP2 = vf32(atop("tmp2", 512), 128)
            ADEC = vf32(atop("adec", 512), 128)
            AFAC = vbf(atop("afac", 2048), SEG)
            WE = vbf(atop("we", 2048), SEG)
            WW = vbf(atop("ww", 2048), SEG)
            DECB = vbf(atop("decb", 2176), 8 + SEG)
            VV = vbf(atop("vv", 2176), 8 + SEG)
            UB = [vbf(atop("ub" + str(i), 2176), 8 + SEG) for i in range(2)]
            TRM = [vbf(atop("trm" + str(i), 2048), SEG) for i in range(2)]
            hst16_off = atop("hst16", 16 * 256)
            HST16 = [vbf(hst16_off + i * 256, 128) for i in range(16)]
            wts8_off = atop("wts8", 8 * 256)
            WTS8 = [vbf(wts8_off + tt * 256, 128) for tt in range(8)]
            EV1 = vf32(atop("ev1", 2048), 512)
            EV2 = vf32(atop("ev2", 2048), 512)
            SLZ = vbf(atop("slz", 1024), 512)
            SLZ2 = vbf(atop("slz2", 1024), 512)
            EV1b = vf32(atop("ev1b", 2048), 512)
            EV2b = vf32(atop("ev2b", 2048), 512)
            ZSG = [vbf(atop("zsg" + str(i), 2048), SEG) for i in range(2)]
            XNSG = [vbf(atop("xnsg" + str(i), 2048), SEG) for i in range(2)]
            HSG = [vbf(atop("hsg" + str(i), 2048), SEG) for i in range(2)]
            MST = vbf(atop("mst", 1024), 512)
            CFM = vbf(atop("cfm", 2080), 8 + SEG + 8)
            BFM = vbf(atop("bfm", 2080), 8 + SEG + 8)
            GPROD = vbf(atop("gprod", 2048), SEG)
            GROWB = vbf(atop("growb", 2048), SEG)[0:1, :]
            BTM = vbf(atop("btm", 256), 128)
            bstk_off = atop("bstk", 8 * 512)
            BSTK = [vbf(bstk_off + tt * 512, 256) for tt in range(8)]
            cstk16_off = atop("cstk16", 16 * 256)
            CSTK16 = [vbf(cstk16_off + i * 256, 128) for i in range(16)]
            ONE_T = vf32(atop("one_t", 512), 128)
            ONEF = vf32(atop("onef", 64), 1)
            CARRY = vf32(atop("carry", 512), 128)

            P.wait("gpsimd", sDs, h_spilled)
            for cc in range(NDC):
                dma_g(lambda e, cc=cc: e.dma_start(out=DTW[cc], in_=dt_w[cc * 128:(cc + 1) * 128, :]))
                dma_g(lambda e, cc=cc: e.dma_start(out=WOUT[cc], in_=w_out[cc * 128:(cc + 1) * 128, :]))
                dma_g(lambda e, cc=cc: e.dma_start(out=BWT[cc], in_=b_w[cc * 128:(cc + 1) * 128, :]))
                dma_g(lambda e, cc=cc: e.dma_start(out=CWT[cc], in_=c_w[cc * 128:(cc + 1) * 128, :]))
            mweights = cDg.v
            P.wait("vector", sDg, mweights)
            v_op(lambda e: e.memset(ONE_T, 1.0))
            v_op(lambda e: e.memset(ONEF, 1.0))
            v_op(lambda e: e.memset(CARRY, 0.0))
            v_op(lambda e: e.memset(VV[:, 0:8], 0.0))
            v_op(lambda e: e.memset(DECB[:, 0:8], 0.0))
            v_op(lambda e: e.memset(UB[0][:, 0:8], 0.0))
            v_op(lambda e: e.memset(UB[1][:, 0:8], 0.0))
            v_op(lambda e: e.memset(CFM[:, 0:8], 0.0))
            v_op(lambda e: e.memset(CFM[:, 8 + SEG:], 0.0))
            v_op(lambda e: e.memset(BFM[:, 0:8], 0.0))
            v_op(lambda e: e.memset(BFM[:, 8 + SEG:], 0.0))
            minit_v = cV.v
            P.wait("tensor", sV, minit_v)
            P.wait("scalar", sV, minit_v)

            for sg in range(NSEG):
                so = sg * SEG
                P.wait("sync", sT, cT.v)
                P.wait("sync", sV, cV.v)
                for cc in range(NDC):
                    dma_s(lambda e, cc=cc, so=so: e.dma_start(
                        out=XSG[cc], in_=xs_s[cc * 128:(cc + 1) * 128, so:so + SEG]))
                xs_ld = cDs.v
                P.wait("tensor", sDs, xs_ld)
                for tt in range(8):
                    for cc in range(NDC):
                        t_op(lambda e, tt=tt, cc=cc: e.matmul(
                            psD[:, tt * 16:tt * 16 + 16],
                            XSG[cc][:, tt * 128:(tt + 1) * 128], BWT[cc],
                            start=(tt == 0 and cc == 0), stop=(tt == 7 and cc == NDC - 1)),
                            inc=(cc == NDC - 1))
                tmk = cT.v
                P.wait("vector", sT, tmk)
                v_op(lambda e: e.tensor_copy(BTM, psD[:, 0:128]))
                btm_v = cV.v
                for which, wt, dst in ((0, CWT, CFM), (1, BWT, BFM)):
                    for c2 in range(2):
                        bank = psC[0:16, 0:512]
                        for cc in range(NDC):
                            t_op(lambda e, wt=wt, cc=cc, c2=c2, bank=bank: e.matmul(
                                bank, wt[cc], XSG[cc][:, c2 * 512:(c2 + 1) * 512],
                                start=(cc == 0), stop=(cc == NDC - 1)))
                        tmk = cT.v
                        P.wait("vector", sT, tmk)
                        v_op(lambda e, dst=dst, c2=c2, bank=bank: e.tensor_copy(
                            dst[0:16, 8 + c2 * 512:8 + (c2 + 1) * 512], bank))
                        vmk = cV.v
                        P.wait("tensor", sV, vmk)
                cfm_v = cV.v
                P.wait("vector", sV, btm_v)
                for tt in range(8):
                    v_op(lambda e, tt=tt: e.tensor_mul(
                        BSTK[tt].rearrange("p (a b) -> p a b", b=16),
                        BTM[:, tt * 16:(tt + 1) * 16].rearrange("p (a b) -> p a b", a=1)
                        .broadcast_to((128, 16, 16)),
                        BSTM.rearrange("p (a b) -> p a b", b=16)))
                bstk_v = cV.v
                cstk_marks = {}
                for tt in range(8):
                    slot = (psD if tt % 2 == 0 else psC)[:, 0:128]
                    P.wait("tensor", sV, cstk_marks.get(tt - 2, 0))
                    t_op(lambda e, tt=tt, slot=slot: e.matmul(
                        slot, REPM, CFM[0:16, 8 + tt * 128:8 + (tt + 1) * 128],
                        start=True, stop=True))
                    tmk = cT.v
                    P.wait("vector", sT, tmk)
                    for hf in range(2):
                        v_op(lambda e, tt=tt, hf=hf, slot=slot: e.tensor_mul(
                            CSTK16[tt * 2 + hf], slot,
                            CSTM[:, hf * 128:(hf + 1) * 128]))
                    cstk_marks[tt] = cV.v
                cstk_v = cV.v
                for k in range(LSUB):
                    v_op(lambda e, k=k: e.tensor_mul(
                        GPROD[0:16, :], CFM[0:16, 8:8 + SEG], BFM[0:16, 8 - k:8 - k + SEG]))
                    vmk = cV.v
                    P.wait("tensor", sV, vmk)
                    for c2 in range(2):
                        t_op(lambda e, c2=c2: e.matmul(
                            psC[0:1, 0:512], ONESC[0:16, :], GPROD[0:16, c2 * 512:(c2 + 1) * 512],
                            start=True, stop=True))
                        tmk = cT.v
                        P.wait("vector", sT, tmk)
                        v_op(lambda e, k=k, c2=c2: e.tensor_mul(
                            GROWB[:, c2 * 512:(c2 + 1) * 512].rearrange("p (a b) -> p a b", b=LSUB),
                            psC[0:1, 0:512].rearrange("p (a b) -> p a b", b=LSUB),
                            GMASK[:, k * LSUB:(k + 1) * LSUB].rearrange(
                                "p (a b) -> p a b", a=1).broadcast_to((1, 64, LSUB))))
                        vmk = cV.v
                        P.wait("tensor", sV, vmk)
                        t_op(lambda e, c2=c2: e.matmul(
                            psB[:, c2 * 512:(c2 + 1) * 512], ONES1,
                            GROWB[:, c2 * 512:(c2 + 1) * 512], start=True, stop=True))
                        tmk = cT.v
                        P.wait("vector", sT, tmk)
                        v_op(lambda e, k=k, c2=c2: e.tensor_copy(
                            GBC[k][:, c2 * 512:(c2 + 1) * 512], psB[:, c2 * 512:(c2 + 1) * 512]))
                        vmk = cV.v
                        P.wait("tensor", sV, vmk)
                g_done_v = cV.v

                for dc in range(NDC):
                    P.wait("sync", sV, cV.v)
                    dma_s(lambda e, dc=dc, so=so: e.dma_start(
                        out=ZSG[dc % 2], in_=z_s[dc * 128:(dc + 1) * 128, so:so + SEG]))
                    z_ld = cDs.v
                    P.wait("tensor", sV, cV.v)
                    for c2 in range(2):
                        for cc in range(NDC):
                            t_op(lambda e, dc=dc, c2=c2, cc=cc: e.matmul(
                                psB[:, c2 * 512:(c2 + 1) * 512],
                                DTW[cc][:, dc * 128:(dc + 1) * 128],
                                XSG[cc][:, c2 * 512:(c2 + 1) * 512],
                                start=(cc == 0), stop=(cc == NDC - 1)), inc=(cc == NDC - 1))
                    dt_t = cT.v
                    P.wait("scalar", sT, dt_t)
                    a_op(lambda e, dc=dc: e.activation(SC_E, psB[:, 0:1024], AF.Exp,
                                                       bias=veccol(3, dc)))
                    amk = cA.v
                    a_op(lambda e: e.activation(DTT, SC_E, AF.Ln, bias=ONEF))
                    sp_a = cA.v
                    P.wait("vector", sAc, sp_a)
                    v_op(lambda e, dc=dc: e.tensor_scalar_mul(LDEC, DTT, NEGA[:, dc:dc + 1]))
                    v_op(lambda e, dc=dc: e.tensor_mul(VV[:, 8:], DTT, XSG[dc]))
                    ld_v = cV.v
                    for ch in range(8):
                        v_op(lambda e, ch=ch: e.tensor_tensor_scan(
                            S128[:, ch * 128:(ch + 1) * 128], ONE_T,
                            LDEC[:, ch * 128:(ch + 1) * 128], 0.0, ALU.mult, ALU.add))
                    s128_v = cV.v
                    v_op(lambda e: e.memset(TMP2.rearrange("p (a b) -> p a b", b=16)[:, :, 0:1], 0.0))
                    v_op(lambda e: e.tensor_copy(
                        TMP2.rearrange("p (c j) -> p c j", j=16)[:, :, 1:16],
                        S128.rearrange("p (c j) -> p c j", j=128)[:, :, 7:120:8]))
                    tmp2_v = cV.v
                    v_op(lambda e: e.tensor_sub(
                        SC1, S128.rearrange("p (a b) -> p a b", b=8)[:, :, 7:8]
                        .rearrange("p a b -> p (a b)"), TMP2))
                    vmk = cV.v
                    P.wait("scalar", sV, vmk)
                    a_op(lambda e: e.activation(ADEC, SC1, AF.Exp))
                    adec_a = cA.v
                    v_op(lambda e: e.tensor_sub(
                        SC_E.rearrange("p (a b) -> p a b", b=8),
                        S128.rearrange("p (a b) -> p a b", b=8),
                        TMP2.rearrange("p (a b) -> p a b", b=1).broadcast_to((128, 128, 8))))
                    vmk = cV.v
                    P.wait("scalar", sV, vmk)
                    a_op(lambda e: e.activation(AFAC, SC_E, AF.Exp))
                    afac_a = cA.v
                    P.wait("vector", sAc, afac_a)
                    v_op(lambda e: e.tensor_sub(
                        SC_E.rearrange("p (a b) -> p a b", b=8),
                        S128.rearrange("p (a b) -> p a b", b=8)[:, :, 7:8]
                        .broadcast_to((128, 128, 8)),
                        S128.rearrange("p (a b) -> p a b", b=8)))
                    vmk = cV.v
                    P.wait("scalar", sV, vmk)
                    a_op(lambda e: e.activation(WE, SC_E, AF.Exp))
                    a_op(lambda e: e.activation(DECB[:, 8:], LDEC, AF.Exp))
                    we_a = cA.v
                    P.wait("vector", sAc, we_a)
                    v_op(lambda e: e.tensor_mul(WW, WE, VV[:, 8:]))
                    ww_v = cV.v

                    P.wait("tensor", sAc, we_a)
                    P.wait("tensor", sV, ww_v)
                    lag_marks = {}
                    v_op(lambda e: e.tensor_mul(TRM[0], VV[:, 8:], GBC[0]))
                    t0_v = cV.v
                    P.wait("tensor", sV, t0_v)
                    for c2 in range(2):
                        t_op(lambda e, c2=c2: e.matmul(
                            psA[:, c2 * 512:(c2 + 1) * 512], IDENT,
                            TRM[0][:, c2 * 512:(c2 + 1) * 512],
                            start=True, stop=False), inc=(c2 == 1))
                    lag_marks[0] = cT.v
                    for k in range(1, LSUB):
                        src_u = VV if k == 1 else UB[(k - 1) % 2]
                        v_op(lambda e, k=k, src_u=src_u: e.tensor_mul(
                            UB[k % 2][:, 8:], DECB[:, 8:], src_u[:, 7:7 + SEG]))
                        if k >= 2:
                            P.wait("vector", sT, lag_marks[k - 2])
                        v_op(lambda e, k=k: e.tensor_mul(TRM[k % 2], UB[k % 2][:, 8:], GBC[k]))
                        vmk = cV.v
                        P.wait("tensor", sV, vmk)
                        for c2 in range(2):
                            t_op(lambda e, k=k, c2=c2: e.matmul(
                                psA[:, c2 * 512:(c2 + 1) * 512], IDENT,
                                TRM[k % 2][:, c2 * 512:(c2 + 1) * 512],
                                start=False, stop=(k == LSUB - 1)), inc=(c2 == 1))
                        lag_marks[k] = cT.v
                    y_t = cT.v

                    psBb = psB.bitcast(BF16)
                    for tt in range(8):
                        t_op(lambda e, tt=tt: e.transpose(
                            psBb[:, tt * 128:(tt + 1) * 128],
                            WW[:, tt * 128:(tt + 1) * 128], IDENT), inc=(tt == 7))
                    tmk = cT.v
                    P.wait("vector", sT, tmk)
                    for tt in range(8):
                        v_op(lambda e, tt=tt: e.tensor_copy(
                            WTS8[tt], psBb[:, tt * 128:(tt + 1) * 128]), inc=(tt == 7))
                    wts_v = cV.v
                    P.wait("tensor", sV, wts_v)
                    b8_slot_mark = {}
                    for tt in range(8):
                        slot = psB[:, (tt % 2) * 512:(tt % 2) * 512 + 256]
                        P.wait("tensor", sV, b8_slot_mark.get(tt - 2, 0))
                        t_op(lambda e, tt=tt, slot=slot: e.matmul(
                            slot, WTS8[tt], BSTK[tt], start=True, stop=True))
                        tmk = cT.v
                        P.wait("vector", sT, tmk)
                        v_op(lambda e, tt=tt, slot=slot: e.tensor_copy(
                            B8S[:, tt * 256:(tt + 1) * 256], slot))
                        b8_slot_mark[tt] = cV.v
                    v_op(lambda e, dc=dc: e.tensor_copy(HSUB[:, 0:16], CARRY[:, dc * 16:(dc + 1) * 16]))
                    P.wait("scalar", sV, cV.v)
                    for si in range(16):
                        v_op(lambda e, si=si: e.tensor_tensor_scan(
                            HSUB[:, 16 + si:16 + si + 2033 - 1:16].rearrange("p a -> p a") if False else
                            HSUB[:, 16:2064].rearrange("p (a b) -> p b a", b=16)[:, si, :],
                            ADEC,
                            B8S.rearrange("p (a b) -> p b a", b=16)[:, si, :],
                            HSUB[:, si:si + 1], ALU.mult, ALU.add))
                    scan_v = cV.v
                    v_op(lambda e, dc=dc: e.tensor_copy(CARRY[:, dc * 16:(dc + 1) * 16],
                                                        HSUB[:, 2048:2064]))
                    carry_v = cV.v

                    P.wait("tensor", sV, carry_v)
                    ro_tr = {}
                    ro_ev = {}
                    for u in range(16):
                        slot = (psC if u % 2 == 0 else psD)[:, ((u // 2) % 2) * 128:((u // 2) % 2) * 128 + 128]
                        P.wait("tensor", sV, ro_ev.get(u - 2, 0))
                        t_op(lambda e, u=u, slot=slot: e.transpose(
                            slot, HSUB[:, u * 128:(u + 1) * 128], IDENTF))
                        ro_tr[u] = cT.v
                        P.wait("vector", sT, ro_tr[u])
                        v_op(lambda e, u=u, slot=slot: e.tensor_copy(HST16[u], slot))
                        ro_ev[u] = cV.v
                    P.wait("tensor", sV, ro_ev[15])
                    for u in range(16):
                        tt, hf = u // 2, u % 2
                        t_op(lambda e, u=u, tt=tt, hf=hf: e.matmul(
                            psA[:, 1024 + tt * 128:1024 + (tt + 1) * 128],
                            HST16[u], CSTK16[u],
                            start=(hf == 0 and tt % 4 == 0),
                            stop=(hf == 1 and tt % 4 == 3)), inc=(u == 15))
                    z_t = cT.v

                    P.wait("vector", sT, z_t)
                    P.wait("vector", sDs, z_ld)
                    P.wait("scalar", sDs, z_ld)
                    a_op(lambda e, dc=dc: e.activation(
                        SLZ, ZSG[dc % 2][:, 0:512], AF.Silu))
                    a_op(lambda e, dc=dc: e.activation(
                        SLZ2, ZSG[dc % 2][:, 512:1024], AF.Silu))
                    slz_a = cA.v
                    v_op(lambda e: e.tensor_mul(EV1, psA[:, 1024:1536], AFAC[:, 0:512]))
                    v_op(lambda e: e.tensor_mul(EV1b, psA[:, 1536:2048], AFAC[:, 512:1024]))
                    v_op(lambda e: e.tensor_add(EV2, psA[:, 0:512], EV1))
                    v_op(lambda e: e.tensor_add(EV2b, psA[:, 512:1024], EV1b))
                    v_op(lambda e, dc=dc: e.scalar_tensor_tensor(
                        EV1, XSG[dc][:, 0:512], veccol(5, dc), EV2, ALU.mult, ALU.add))
                    v_op(lambda e, dc=dc: e.scalar_tensor_tensor(
                        EV1b, XSG[dc][:, 512:1024], veccol(5, dc), EV2b, ALU.mult, ALU.add))
                    P.wait("vector", sAc, slz_a)
                    v_op(lambda e, dc=dc: e.tensor_mul(YG[dc][:, 0:512], EV1, SLZ))
                    v_op(lambda e, dc=dc: e.tensor_mul(YG[dc][:, 512:1024], EV1b, SLZ2))
                    evac_v2 = cV.v
                    P.wait("tensor", sV, evac_v2)
                    if sg == 0 and dc == 0 and "m1" in dbg:
                        P.wait("sync", sV, cV.v)
                        P.wait("sync", sAc, cA.v)
                        P.wait("sync", sT, cT.v)
                        v_op(lambda e: e.tensor_copy(SC_E, psA[:, 0:1024]))
                        P.wait("sync", sV, cV.v)
                        apy = dbg_t("m_ypsum", [128, 1024])
                        dma_s(lambda e, apy=apy: e.dma_start(out=apy, in_=SC_E))
                        P.wait("vector", sDs, cDs.v)
                        v_op(lambda e: e.tensor_copy(SC_E, psA[:, 1024:2048]))
                        P.wait("sync", sV, cV.v)
                        apz = dbg_t("m_zpsum", [128, 1024])
                        dma_s(lambda e, apz=apz: e.dma_start(out=apz, in_=SC_E))
                        P.wait("vector", sDs, cDs.v)
                        taps = [("m_dt", DTT, F32, SEG), ("m_s128", S128, F32, SEG),
                                ("m_vv", VV, BF16, 8 + SEG), ("m_adec", ADEC, F32, 128),
                                ("m_afac", AFAC, BF16, SEG), ("m_ww", WW, BF16, SEG),
                                ("m_b8", B8S, F32, 2048), ("m_hsub", HSUB, F32, 2064),
                                ("m_yg", YG[0], BF16, SEG), ("m_tmp2", TMP2, F32, 128)]
                        for nm, tile, dty, ncols in taps:
                            ap = dbg_t(nm, [128, ncols], dty)
                            dma_s(lambda e, ap=ap, tile=tile, ncols=ncols: e.dma_start(
                                out=ap, in_=tile[:, 0:ncols]))
                        hold = cDs.v
                        for eng in ("vector", "scalar", "tensor", "gpsimd"):
                            P.wait(eng, sDs, hold)

                dma_s(lambda e, so=so: e.dma_start(out=XNSG[0], in_=xn_s[0:128, so:so + SEG]))
                dma_s(lambda e, so=so: e.dma_start(out=HSG[0], in_=h_s[0:128, so:so + SEG]))
                str_loads = [cDs.v, 0]
                str_used = [0, 0]
                for mc in range(NDC):
                    if mc < NDC - 1:
                        P.wait("sync", sV, str_used[(mc + 1) % 2])
                        dma_s(lambda e, mc=mc, so=so: e.dma_start(
                            out=XNSG[(mc + 1) % 2], in_=xn_s[(mc + 1) * 128:(mc + 2) * 128, so:so + SEG]))
                        dma_s(lambda e, mc=mc, so=so: e.dma_start(
                            out=HSG[(mc + 1) % 2], in_=h_s[(mc + 1) * 128:(mc + 2) * 128, so:so + SEG]))
                        str_loads[(mc + 1) % 2] = cDs.v
                    P.wait("vector", sDs, str_loads[mc % 2])
                    for c2 in range(2):
                        bank = psB[:, c2 * 512:(c2 + 1) * 512]
                        for cc in range(NDC):
                            t_op(lambda e, mc=mc, c2=c2, cc=cc, bank=bank: e.matmul(
                                bank, WOUT[cc][:, mc * 128:(mc + 1) * 128],
                                YG[cc][:, c2 * 512:(c2 + 1) * 512],
                                start=(cc == 0), stop=(cc == NDC - 1)))
                        tmk = cT.v
                        P.wait("vector", sT, tmk)
                        v_op(lambda e, mc=mc, c2=c2, bank=bank: e.tensor_add(
                            EV2, bank, XNSG[mc % 2][:, c2 * 512:(c2 + 1) * 512]))
                        v_op(lambda e, mc=mc, c2=c2: e.tensor_add(
                            MST, EV2, HSG[mc % 2][:, c2 * 512:(c2 + 1) * 512]))
                        vmk = cV.v
                        P.wait("tensor", sV, vmk)
                        P.wait("sync", sV, vmk)
                        dma_s(lambda e, mc=mc, c2=c2, so=so: e.dma_start(
                            out=macc_s[mc * 128:(mc + 1) * 128, so + c2 * 512:so + (c2 + 1) * 512],
                            in_=MST))
                        P.wait("vector", sDs, cDs.v)
                    str_used[mc % 2] = cV.v
            mamba_done = cDs.v

            if "maccd" in dbg:
                ap = dbg_t("maccd", [D, T], BF16)
                P.wait("sync", sDs, mamba_done)
                dma_s(lambda e, ap=ap: e.dma_start(out=ap, in_=macc_s))

        if stop_after is None:
            for nm in ("dtw", "wout", "bcw", "xseg", "yg", "gbc", "b8s", "hsub",
                       "sc_e", "dtt", "ldec", "s128", "sc1", "tmp2", "adec", "afac",
                       "we", "ww", "decb", "vv", "ub0", "ub1", "trm0", "trm1",
                       "hst16", "cstk16", "bstk", "wts8", "ev1", "ev2",
                       "slz", "slz2", "ev1b", "ev2b", "zsg0", "zsg1", "xnsg0", "xnsg1",
                       "hsg0", "hsg1", "mst", "cfm", "bfm", "gprod", "growb", "btm",
                       "one_t", "onef", "carry"):
                ar.free(nm)
            hroff = ar.alloc("hres", 64 * 1024)
            HRES = [vbf(hroff + cc * 8192, T) for cc in range(NDC)]
            accoff = ar.alloc("acc", 64 * 1024)
            ACC = [vbf(accoff + tc * 2048, 1024) for tc in range(32)]
            ewoff = ar.alloc("ew", 32 * 1024)
            EW = [[vbf(ewoff + b * 16384 + cc * 2048, 1024) for cc in range(NDC)] for b in range(2)]
            gwoff = ar.alloc("gwt", 1024, top=True)
            GWT = [vbf(gwoff + cc * 32, NE) for cc in range(NDC)]
            cmboff = ar.alloc("comb", 2048, top=True)
            COMBT = vf32(cmboff, 512)
            def gsc(name, cols):
                return vf32(ar.alloc(name, cols * 4, top=True), cols)
            LG = [gsc("lg" + str(i), 8) for i in range(2)]
            LM = [gsc("lm" + str(i), 8) for i in range(2)]
            E1 = [gsc("e1_" + str(i), 8) for i in range(2)]
            E2 = [gsc("e2_" + str(i), 8) for i in range(2)]
            C1 = [gsc("c1_" + str(i), 8) for i in range(2)]
            M1 = [gsc("m1_" + str(i), 1) for i in range(2)]
            M2 = [gsc("m2_" + str(i), 1) for i in range(2)]
            SD = [gsc("sd" + str(i), 1) for i in range(2)]
            W1 = [gsc("w1_" + str(i), 1) for i in range(2)]
            W2 = [gsc("w2_" + str(i), 1) for i in range(2)]

            P.wait("gpsimd", sDs, mamba_done)
            P.wait("gpsimd", sV, cV.v)
            for cc in range(NDC):
                dma_g(lambda e, cc=cc: e.dma_start(out=HRES[cc], in_=h_s[cc * 128:(cc + 1) * 128, :]))
                dma_g(lambda e, cc=cc: e.dma_start(out=GWT[cc], in_=gate_t[cc * 128:(cc + 1) * 128, :]))
            h_ld = cDg.v
            P.wait("vector", sDg, h_ld)
            for cc in range(NDC):
                v_op(lambda e, cc=cc: e.tensor_scalar_mul(GWT[cc], GWT[cc], veccol(7, cc)))
            gw_v = cV.v
            P.wait("tensor", sV, gw_v)

            for tp in range(16):
                tcs = (2 * tp, 2 * tp + 1)
                for i, tc in enumerate(tcs):
                    bank = psC if i == 0 else psD
                    for cc in range(NDC):
                        t_op(lambda e, tc=tc, cc=cc, bank=bank: e.matmul(
                            bank[:, 0:8], HRES[cc][:, tc * 128:(tc + 1) * 128], GWT[cc],
                            start=(cc == 0), stop=(cc == NDC - 1)), inc=(cc == NDC - 1))
                tmk = cT.v
                P.wait("vector", sT, tmk)
                for i, tc in enumerate(tcs):
                    bank = psC if i == 0 else psD
                    v_op(lambda e, i=i, tc=tc, bank=bank: e.tensor_scalar_mul(
                        LG[i], bank[:, 0:8], scol(S_RG, tc)))
                for i in range(2):
                    v_op(lambda e, i=i: e.reduce_max(M1[i], LG[i], axis=AX))
                for i in range(2):
                    v_op(lambda e, i=i: e.tensor_single_scalar(E1[i], LG[i], M1[i], ALU.is_equal))
                for i in range(2):
                    v_op(lambda e, i=i: e.scalar_tensor_tensor(
                        LM[i], E1[i], -1e30, LG[i], ALU.mult, ALU.add))
                for i in range(2):
                    v_op(lambda e, i=i: e.reduce_max(M2[i], LM[i], axis=AX))
                for i in range(2):
                    v_op(lambda e, i=i: e.tensor_sub(SD[i], M1[i], M2[i]))
                vmk = cV.v
                P.wait("scalar", sV, vmk)
                for i in range(2):
                    a_op(lambda e, i=i: e.activation(W1[i], SD[i], AF.Sigmoid))
                amk = cA.v
                P.wait("vector", sAc, amk)
                for i, tc in enumerate(tcs):
                    v_op(lambda e, i=i, tc=tc: e.tensor_mul(W1[i], W1[i], scol(S_RG, tc)))
                for i, tc in enumerate(tcs):
                    v_op(lambda e, i=i, tc=tc: e.tensor_sub(W2[i], scol(S_RG, tc), W1[i]))
                for i in range(2):
                    v_op(lambda e, i=i: e.tensor_single_scalar(E2[i], LG[i], M2[i], ALU.is_equal))
                for i in range(2):
                    v_op(lambda e, i=i: e.tensor_scalar_mul(C1[i], E1[i], W1[i][:, 0:1]))
                for i, tc in enumerate(tcs):
                    v_op(lambda e, i=i, tc=tc: e.scalar_tensor_tensor(
                        COMBT[:, tc * 8:(tc + 1) * 8], E2[i], W2[i][:, 0:1], C1[i],
                        ALU.mult, ALU.add))
                gv = cV.v
                P.wait("tensor", sV, gv)
            gating_v = cV.v

            P.wait("gpsimd", sT, cT.v)
            for cc in range(NDC):
                dma_g(lambda e, cc=cc: e.dma_start(
                    out=EW[0][cc], in_=expert_t[0, cc * 128:(cc + 1) * 128, :]))
            ew_loads = [cDg.v, 0]
            ew_used = [0, 0]
            moe_evac = {}
            for e_i in range(NE):
                if e_i < NE - 1:
                    P.wait("gpsimd", sT, ew_used[(e_i + 1) % 2])
                    for cc in range(NDC):
                        dma_g(lambda e, e_i=e_i, cc=cc: e.dma_start(
                            out=EW[(e_i + 1) % 2][cc],
                            in_=expert_t[e_i + 1, cc * 128:(cc + 1) * 128, :]))
                    ew_loads[(e_i + 1) % 2] = cDg.v
                P.wait("vector", sDg, ew_loads[e_i % 2])
                for cc in range(NDC):
                    v_op(lambda e, e_i=e_i, cc=cc: e.tensor_scalar_mul(
                        EW[e_i % 2][cc], EW[e_i % 2][cc], veccol(7, cc)))
                sc_v = cV.v
                P.wait("tensor", sV, sc_v)
                for tc in range(32):
                    gidx = e_i * 32 + tc
                    pbank = psB[:, 0:1024] if gidx % 2 == 0 else psA[:, 0:1024]
                    P.wait("tensor", sV, moe_evac.get(gidx - 2, 0))
                    for c2 in range(2):
                        for cc in range(NDC):
                            t_op(lambda e, e_i=e_i, tc=tc, c2=c2, cc=cc, pbank=pbank: e.matmul(
                                pbank[:, c2 * 512:(c2 + 1) * 512],
                                HRES[cc][:, tc * 128:(tc + 1) * 128],
                                EW[e_i % 2][cc][:, c2 * 512:(c2 + 1) * 512],
                                start=(cc == 0), stop=(cc == NDC - 1)),
                                inc=(cc == NDC - 1 and c2 == 1))
                    tmk = cT.v
                    P.wait("vector", sT, tmk)
                    if e_i == 0:
                        v_op(lambda e, tc=tc, pbank=pbank: e.tensor_scalar_mul(
                            ACC[tc], pbank, COMBT[:, tc * 8 + 0:tc * 8 + 1]))
                    else:
                        v_op(lambda e, e_i=e_i, tc=tc, pbank=pbank: e.scalar_tensor_tensor(
                            ACC[tc], pbank, COMBT[:, tc * 8 + e_i:tc * 8 + e_i + 1],
                            ACC[tc], ALU.mult, ALU.add))
                    moe_evac[gidx] = cV.v
                ew_used[e_i % 2] = cT.v
            moe_v = cV.v
            ar.free("hres")
            ar.free("ew")

            mrdoff = ar.alloc("mrd", 4 * 1024, top=True)
            MRD = [vbf(mrdoff + i * 1024, 512) for i in range(2)]
            xrdoff = ar.alloc("xrd", 8 * 1024, top=True)
            XRD = [vf32(xrdoff + i * 2048, 512) for i in range(2)]
            ostoff = ar.alloc("ostg", 8 * 1024, top=True)
            OSTG = [vf32(ostoff + i * 2048, 512) for i in range(2)]
            fscoff = ar.alloc("fsc", 8 * 1024, top=True)
            FSC = [vf32(fscoff + i * 2048, 512) for i in range(2)]

            P.wait("sync", sV, moe_v)
            ld_marks = {}
            out_marks = {}
            for dc in range(NDC):
                for gq in range(8):
                    i = (dc * 8 + gq) % 2
                    if dc * 8 + gq >= 2:
                        P.wait("sync", sV, ld_marks.get(dc * 8 + gq - 2, 0))
                        P.wait("sync", sP, cP.v - 1)
                    dma_s(lambda e, dc=dc, gq=gq, i=i: e.dma_start(
                        out=MRD[i], in_=macc_s[dc * 128:(dc + 1) * 128, gq * 512:(gq + 1) * 512]))
                    dma_s(lambda e, dc=dc, gq=gq, i=i: e.dma_start(
                        out=XRD[i], in_=x_fm[dc * 128:(dc + 1) * 128, gq * 512:(gq + 1) * 512]))
                    dmk = cDs.v
                    bank = (psD if i == 0 else psC).bitcast(BF16)
                    P.wait("tensor", sV, cV.v)
                    if dc * 8 + gq >= 2:
                        P.wait("gpsimd", sDs, out_marks.get(dc * 8 + gq - 2, 0))
                    for j in range(4):
                        t_op(lambda e, dc=dc, gq=gq, j=j, bank=bank: e.transpose(
                            bank[:, j * 128:(j + 1) * 128],
                            ACC[gq * 4 + j][:, dc * 128:(dc + 1) * 128], IDENT),
                            inc=(j == 3))
                    tmk = cT.v
                    P.wait("vector", sT, tmk)
                    P.wait("vector", sDs, dmk)
                    v_op(lambda e, i=i, bank=bank: e.tensor_add(FSC[i], bank[:, 0:512], MRD[i]))
                    vmk = cV.v
                    P.wait("gpsimd", sV, vmk)
                    p_op(lambda e, i=i: e.tensor_add(OSTG[i], FSC[i], XRD[i]))
                    pmk = cP.v
                    ld_marks[dc * 8 + gq] = vmk
                    P.wait("sync", sP, pmk)
                    dma_s(lambda e, dc=dc, gq=gq, i=i: e.dma_start(
                        out=out_fm[dc * 128:(dc + 1) * 128, gq * 512:(gq + 1) * 512],
                        in_=OSTG[i]))
                    out_marks[dc * 8 + gq] = cDs.v
        if "k0" in dbg:
            ap = dbg_t("k0", [128, T], BF16)
            P.wait("sync", sV, qkv_done_v)
            dma_s(lambda e, ap=ap: e.dma_start(out=ap, in_=KF[0]))
            if "k1" in dbg:
                ap2 = dbg_t("k1", [128, T], BF16)
                dma_s(lambda e, ap2=ap2: e.dma_start(out=ap2, in_=KF[1]))
        if "vt0" in dbg:
            ap = dbg_t("vt0", [128, 260], BF16)
            P.wait("sync", sV, qkv_done_v)
            dma_s(lambda e, ap=ap: e.dma_start(out=ap, in_=VT[0]))
            if "vt31" in dbg:
                ap2 = dbg_t("vt31", [128, 260], BF16)
                dma_s(lambda e, ap2=ap2: e.dma_start(out=ap2, in_=VT[31]))
        if "qs" in dbg:
            ap = dbg_t("qs", [D, T], BF16)
            P.wait("sync", sDs, winproj_done)
            dma_s(lambda e, ap=ap: e.dma_start(out=ap, in_=q_s))
        if "xsp" in dbg:
            ap = dbg_t("xsp", [D, T], BF16)
            P.wait("sync", sDs, winproj_done)
            dma_s(lambda e, ap=ap: e.dma_start(out=ap, in_=xsp_s))
        if "zs" in dbg:
            ap = dbg_t("zs", [D, T], BF16)
            P.wait("sync", sDs, winproj_done)
            dma_s(lambda e, ap=ap: e.dma_start(out=ap, in_=z_s))
        if "xss" in dbg:
            ap = dbg_t("xss", [D, T], BF16)
            P.wait("sync", sDs, conv_done)
            dma_s(lambda e, ap=ap: e.dma_start(out=ap, in_=xs_s))

        if "stats" in dbg:
            ap = dbg_t("stats", [128, NSLOT * 32])
            P.wait("sync", sV, stats_done)
            dma_s(lambda e, ap=ap: e.dma_start(out=ap, in_=statsF))
        if "xn0" in dbg:
            ap = dbg_t("xn0", [128, T], BF16)
            P.wait("sync", sV, xn_done)
            dma_s(lambda e, ap=ap: e.dma_start(out=ap, in_=XN[0]))
        if "xn7" in dbg:
            ap = dbg_t("xn7", [128, T], BF16)
            P.wait("sync", sV, xn_done)
            dma_s(lambda e, ap=ap: e.dma_start(out=ap, in_=XN[7]))

        if "stage1" in dbg:
            P.wait("gpsimd", sV, xn_done)
            for dc in range(NDC):
                dma_g(lambda e, dc=dc: e.dma_start(
                    out=out_fm[dc * 128:(dc + 1) * 128, :], in_=XN[dc]))

        P.emit("sync", lambda e, v=cDs.v: e.wait_ge(sDs, v))
        P.emit("gpsimd", lambda e, v=cDg.v: e.wait_ge(sDg, v))

        block = st.enter_context(nc.Block())
        for eng in ENGINES:
            fns = P.q[eng]
            if not fns:
                continue
            dec = getattr(block, eng)

            def section(h, fns=fns):
                for f in fns:
                    f(h)
            dec(section)

    return nc, dbg_list


def _prep_inputs(inputs):
    g = {}
    g["w_in"] = np.ascontiguousarray(inputs["L0_attn_W_in"], np.float32)
    g["dt_w"] = np.ascontiguousarray(inputs["L0_attn_dt_w"], np.float32)
    g["b_w"] = np.ascontiguousarray(inputs["L0_attn_B_w"], np.float32)
    g["c_w"] = np.ascontiguousarray(inputs["L0_attn_C_w"], np.float32)
    g["w_out"] = np.ascontiguousarray(inputs["L0_attn_W_out"], np.float32)
    pairs = [(0, 4), (1, 5), (2, 6), (3, 7), (8, 12), (9, 13), (10, 14), (11, 15)]
    hperm = []
    for a, b in pairs:
        hperm.extend(range(a * 64, a * 64 + 64))
        hperm.extend(range(b * 64, b * 64 + 64))
    hperm = np.asarray(hperm)
    wq_p = inputs["L0_ffn_wq"][hperm, :]
    g["wqkv_t"] = np.ascontiguousarray(
        np.concatenate([wq_p.T, inputs["L0_ffn_wk"].T, inputs["L0_ffn_wv"].T], axis=1),
        np.float32)
    g["wo_t"] = np.ascontiguousarray(inputs["L0_ffn_wo"].T[hperm, :], np.float32)
    g["gate_t"] = np.ascontiguousarray(inputs["L0_ffn_gate_w"].T, np.float32)
    g["expert_t"] = np.ascontiguousarray(np.transpose(inputs["L0_ffn_expert_w"], (0, 2, 1)), np.float32)

    def pack(v):
        return np.asarray(v, np.float32).reshape(NDC, 128).T
    vp = np.concatenate([
        pack(inputs["L0_attn_ln_g"]), pack(inputs["L0_attn_ln_b"]),
        pack(inputs["L0_attn_conv_b"]), pack(inputs["L0_attn_dt_b"]),
        pack(inputs["L0_attn_log_A"]), pack(inputs["L0_attn_D_param"]),
        pack(inputs["L0_ffn_attn_norm_w"]), pack(inputs["L0_ffn_ffn_norm_w"]),
    ], axis=1)
    g["vec_pack"] = np.ascontiguousarray(vp, np.float32)
    cw = np.asarray(inputs["L0_attn_conv_w"], np.float32).reshape(D, 4)
    g["conv_w4"] = np.ascontiguousarray(
        cw.reshape(NDC, 128, 4).transpose(1, 0, 2).reshape(128, NDC * 4), np.float32)

    g["ident_bf"] = np.eye(128, dtype=bfdt)
    tau = np.arange(128)
    ii = np.arange(16)
    ss = np.arange(16)
    g["bstack_mask"] = ((tau[:, None] // LSUB) == (np.repeat(ii, 16))[None, :]).astype(bfdt)
    rm = np.zeros((16, 128), dtype=bfdt)
    for j in range(128):
        rm[j % 16, j] = 1
    g["repmat"] = rm
    cm = np.zeros((128, 256), dtype=bfdt)
    tloc = np.arange(128)
    for half in range(2):
        for r in range(128):
            i8, s = r // 16, r % 16
            isub = half * 8 + i8
            cm[r, half * 128:(half + 1) * 128] = ((tloc // LSUB) == isub).astype(np.float32)
    g["cstack_mask"] = cm
    gm = np.zeros((1, LSUB * LSUB), np.float32)
    tt = np.arange(LSUB)
    for k in range(LSUB):
        gm[0, k * LSUB:(k + 1) * LSUB] = (tt >= k).astype(np.float32)
    g["gmask"] = gm
    dmk = np.zeros((128, 4 * 512), bfdt)
    kp = np.arange(128)
    qf = np.arange(512)
    for kc in range(4):
        kglob = kc * 128 + kp
        m = (kglob[:, None] > qf[None, :]).astype(np.float32) * -1e30
        dmk[:, kc * 512:(kc + 1) * 512] = m.astype(bfdt)
    g["diag_mask"] = dmk
    return g


def kernel(**inputs):
    glob = _prep_inputs(inputs)
    x = np.asarray(inputs["L0_attn_x"], np.float32)
    nc, _ = build_nc()
    in_maps = []
    for c in range(8):
        m = dict(glob)
        m["x_fm"] = np.ascontiguousarray(x[c].T)
        in_maps.append(m)
    res = run_bass_kernel_spmd(nc, in_maps, core_ids=list(range(8)))
    out = np.stack([np.ascontiguousarray(res.results[c]["out_fm"].T) for c in range(8)])
    return out
